# version 02
# baseline (speedup 1.0000x reference)
import math
import numpy as np
import ml_dtypes
import concourse.bass as bass
import concourse.mybir as mybir
from concourse.bass_core_rust import IndirectOffsetOnAxis
from concourse.bass_utils import run_bass_kernel_spmd

F32 = mybir.dt.float32
BF16 = mybir.dt.bfloat16
I32 = mybir.dt.int32
AF = mybir.ActivationFunctionType
ALU = mybir.AluOpType
AX = mybir.AxisListType

K_IND = 40000.0
RND_C = 8388608.0


class Cfg:
    def __init__(self, S=8192, ncores=8, D=1024, NH=16, DH=64, NHASH=4, B=2):
        self.S, self.D, self.NH, self.DH, self.NHASH, self.B = S, D, NH, DH, NHASH, B
        self.ncores = ncores
        self.NB = 2 ** ((2 * (S // 64)).bit_length() - 1)
        self.NB2 = 2 * self.NB
        self.JW = NHASH * self.NB
        self.HPC = NH // ncores
        self.UPC = self.HPC * B
        self.T = B * S
        self.TPC = self.T // ncores
        self.GB = S // 128
        self.ROW = 256
        self.nbits = int(math.log2(S))
        self.KD = D // 128
        self.MAC = 512
        self.NMAC = self.T // self.MAC
        self.NSL = self.UPC * NHASH
        self.NBLK = S // 128
        self.NCH = S // 64


def bitonic_schedule(nbits):
    ops = []
    for p in range(1, nbits + 1):
        if p < nbits:
            ops.append(("sign", p))
        for b in range(p - 1, -1, -1):
            ops.append(("cmpx", b))
        if p < nbits:
            ops.append(("sign", p))
    return ops


def np_bitonic(keys, nbits):
    v = keys.astype(np.float32).copy()
    idx = np.arange(1 << nbits)
    for op, a in bitonic_schedule(nbits):
        if op == "sign":
            v = v * (1.0 - 2.0 * ((idx >> a) & 1)).astype(v.dtype)
        else:
            b = 1 << a
            lo = idx[(idx & b) == 0]
            hi = lo | b
            vlo = np.minimum(v[lo], v[hi])
            vhi = np.maximum(v[lo], v[hi])
            v = v.copy()
            v[lo], v[hi] = vlo, vhi
    return v


class Ticked:

    ENGS = ("sync", "gpsimd", "vector", "scalar", "tensor")

    def __init__(self, nc, es):
        self.nc = nc
        self.tkc = es.enter_context(nc.semaphore("tkc"))
        self.tkd = es.enter_context(nc.semaphore("tkd"))
        self.tkg = es.enter_context(nc.semaphore("tkg"))
        self.nc_c = 0
        self.nc_d = 0
        self.nc_g = 0
        self.steps = {e: [] for e in self.ENGS}

    def step(self, eng, fn, dma=False, inc=None):
        wc, wd, wg = self.nc_c, self.nc_d, self.nc_g
        swdge = dma and eng == "gpsimd"
        if dma:
            k = 16 if inc is None else inc
            if swdge:
                self.nc_g += k
            else:
                self.nc_d += k
            auto = False
        else:
            k = 1 if inc is None else inc
            self.nc_c += k
            auto = True
        sem = (self.tkg if swdge else self.tkd) if dma else self.tkc

        def body(engine, fn=fn, wc=wc, wd=wd, wg=wg, sem=sem, k=k, auto=auto):
            if wc:
                engine.wait_ge(self.tkc, wc)
            if wd:
                engine.wait_ge(self.tkd, wd)
            if wg:
                engine.wait_ge(self.tkg, wg)
            last = fn(engine)
            if auto:
                last.then_inc(sem, k)
        self.steps[eng].append(body)

    def run(self, block):
        for name in self.ENGS:
            ops = list(self.steps[name])

            def make(ops):
                def body(eng):
                    for f in ops:
                        f(eng)
                return body
            getattr(block, name)(make(ops))

    def barrier_values(self):
        return self.nc_c, self.nc_d, self.nc_g



def build_attn(cfg: Cfg, debug=(), sort_trunc=None, stop_after_sort=False):
    S, D, DH, NHASH, B = cfg.S, cfg.D, cfg.DH, cfg.NHASH, cfg.B
    UPC, HPC = cfg.UPC, cfg.HPC
    NB, NB2 = cfg.NB, cfg.NB2
    JW, KD, GB, ROW = cfg.JW, cfg.KD, cfg.GB, cfg.ROW
    MAC, NMAC = cfg.MAC, cfg.NMAC
    NSL, nbits, NBLK, NCH = cfg.NSL, cfg.nbits, cfg.NBLK, cfg.NCH
    T = cfg.T
    FB = NSL * GB
    NBK = max(1, JW // 512)
    RVW = min(JW, 512)
    NT = NBLK

    nc = bass.Bass("TRN2", target_bir_lowering=False, detect_race_conditions=False)

    xf = nc.dram_tensor("xf", [D, T], F32, kind="ExternalInput").ap()
    wqkT = nc.dram_tensor("wqkT", [D, HPC * DH], F32, kind="ExternalInput").ap()
    wvT = nc.dram_tensor("wvT", [D, HPC * DH], F32, kind="ExternalInput").ap()
    rots = [nc.dram_tensor(f"rot{h}", [128, JW], F32, kind="ExternalInput").ap()
            for h in range(HPC)]
    sgf = nc.dram_tensor("sgf", [nbits - 1, 128, 128], F32, kind="ExternalInput").ap()
    H_out = nc.dram_tensor("H", [UPC, S, DH], F32, kind="ExternalOutput").ap()

    tab = [nc.dram_tensor(f"tab{u}", [S, ROW], BF16, kind="Internal").ap()
           for u in range(UPC)]
    Obuf = [[nc.dram_tensor(f"Obuf{u}_{n}", [S, 66], F32, kind="Internal").ap()
             for n in range(NHASH)] for u in range(UPC)]

    dbg = {}

    def DBG(name, shape, dt=F32):
        dbg[name] = nc.dram_tensor(name, shape, dt, kind="ExternalOutput").ap()
    for name in debug:
        if name == "comb":
            DBG("dbg_comb", [128, NBLK * 66])
            DBG("dbg_zrec", [128, NBLK])
        elif name == "ph3":
            DBG("dbg_p3escr", [128, 2 * JW], BF16)
            DBG("dbg_p3pscr", [128, 2 * JW], BF16)
            DBG("dbg_p3sm", [128, NHASH])
            DBG("dbg_p3sm2", [128, NHASH])
            DBG("dbg_p3sm3", [128, NHASH])
            DBG("dbg_p3sm4", [128, NHASH])
            DBG("dbg_p3sm5", [128, NHASH])
            DBG("dbg_p3t2", [128, NHASH * NBLK])
            DBG("dbg_p3rv", [128, 512])
            DBG("dbg_p3rv2", [128, 512])
        elif name == "keys":
            DBG("dbg_keys", [128, FB])
        elif name == "skeys":
            DBG("dbg_skeys", [128, FB])
        elif name == "tab":
            DBG("dbg_tab", [UPC, S, ROW], BF16)
        elif name == "stats":
            DBG("dbg_stats", [2, 128, T // 128])
        elif name == "qkfm":
            DBG("dbg_qkfm", [B, 128, S])
        elif name == "sidx":
            DBG("dbg_sidx", [128, FB], I32)
        elif name == "obuf":
            DBG("dbg_obuf", [UPC, NHASH, S, 66])

    import contextlib
    es = contextlib.ExitStack()
    SBUF = lambda *a: es.enter_context(nc.sbuf_tensor(*a))
    PSUM = lambda *a: es.enter_context(nc.psum_tensor(*a))

    with es:
        pA = PSUM("pA", [128, 512], F32)
        pB = PSUM("pB", [128, 512], F32)
        pC = PSUM("pC", [128, 512], F32)
        pD = PSUM("pD", [128, 512], F32)
        pE = PSUM("pE", [128, 512], F32)
        pF = PSUM("pF", [128, 128], F32)
        pG = PSUM("pG", [128, 1024], BF16)
        pH = PSUM("pH", [128, 128], F32)

        sidx = SBUF("sidx", [128, FB], I32)
        iota_col = SBUF("iota_col", [128, 1], F32)
        ones_colb = SBUF("ones_colb", [128, 1], BF16)
        ones_colf = SBUF("ones_colf", [128, 1], F32)
        eye128 = SBUF("eye128", [128, 128], BF16)
        eye128f = SBUF("eye128f", [128, 128], F32)
        iota_row = SBUF("iota_row", [128, 128], F32)
        heads = SBUF("heads", [128, NHASH * ROW], BF16)
        tails = SBUF("tails", [128, NHASH * ROW], BF16)
        theads = SBUF("theads", [128, NHASH * 128], BF16)
        ttails = SBUF("ttails", [128, NHASH * 128], BF16)

        esP = contextlib.ExitStack()
        SBP = lambda *a: esP.enter_context(nc.sbuf_tensor(*a))
        w_qk = SBP("w_qk", [128, KD * HPC * DH], BF16)
        w_qkf = SBP("w_qkf", [128, KD * HPC * DH], F32)
        w_v = SBP("w_v", [128, KD * HPC * DH], BF16)
        nwsum = SBP("nwsum", [1, 2 * HPC * DH], F32)
        rot_s = [SBP(f"rot_s{h}", [128, JW], F32) for h in range(HPC)]
        mu_row = SBP("mu_row", [1, MAC], F32)
        muc = SBP("muc", [128, T // 128], F32)
        sx2c = SBP("sx2c", [128, T // 128], F32)
        r_col = SBP("r_col", [128, T // 128], F32)
        vscr = SBP("vscr", [128, T // 128], F32)
        vscr2 = SBP("vscr2", [128, T // 128], F32)
        FBF = max(FB, NSL * 128)
        keysA = SBP("keysA", [128, FBF], F32)
        keysB = SBP("keysB", [128, FBF], F32)
        sgf_s = SBP("sgf_s", [128, (nbits - 1) * 128], F32)
        ctile = SBP("ctile", [128, JW], BF16)
        t2til = SBP("t2til", [128, NHASH * NBLK], F32)
        mabs = SBP("mabs", [128, NHASH], F32)
        nbias = SBP("nbias", [128, NHASH], F32)
        accP = SBP("accP", [128, NHASH], F32)
        accN = SBP("accN", [128, NHASH], F32)
        rndP = SBP("rndP", [128, NHASH], F32)
        rndN = SBP("rndN", [128, NHASH], F32)
        msel = SBP("msel", [128, NHASH], F32)
        negf = SBP("negf", [128, NHASH], F32)
        kystg = SBP("kystg", [128, NHASH], F32)
        escr = SBP("escr", [128, 2 * JW], BF16)
        pscr = SBP("pscr", [128, 2 * JW], BF16)
        ssqu = SBP("ssqu", [128, NBLK], F32)
        scolb = SBP("scolb", [128, NBLK], F32)

        tk = Ticked(nc, es)
        STEP = tk.step

        def f(g):
            g.iota(iota_col[:], pattern=[[0, 1]], base=0, channel_multiplier=1,
                   allow_small_or_imprecise_dtypes=True)
            g.iota(iota_row[:], pattern=[[1, 128]], base=0, channel_multiplier=0,
                   allow_small_or_imprecise_dtypes=True)
            g.iota(ctile[:].rearrange("p (n j) -> p n j", n=NHASH),
                   pattern=[[0, NHASH], [-1, NB]], base=NB,
                   channel_multiplier=0, allow_small_or_imprecise_dtypes=True)
            return g.iota(t2til[:].rearrange("p (k n) -> p k n", k=NBLK),
                          pattern=[[128, NBLK], [0, NHASH]], base=NB2 * S,
                          channel_multiplier=1, allow_small_or_imprecise_dtypes=True)
        STEP("gpsimd", f)

        def f(g):
            g.dma_start(out=w_qk[:].rearrange("p (k m) -> p k m", k=KD),
                        in_=wqkT.rearrange("(k p) m -> p k m", p=128)
                        ).then_inc(tk.tkg, 16)
            g.dma_start(out=w_v[:].rearrange("p (k m) -> p k m", k=KD),
                        in_=wvT.rearrange("(k p) m -> p k m", p=128)
                        ).then_inc(tk.tkg, 16)
            return None
        STEP("gpsimd", f, dma=True, inc=32)

        def f(s):
            for h in range(HPC):
                s.dma_start(out=rot_s[h][:], in_=rots[h]).then_inc(tk.tkd, 16)
            s.dma_start(out=sgf_s[:].rearrange("p (a b) -> p a b", a=nbits - 1),
                        in_=sgf.rearrange("a p b -> p a b")).then_inc(tk.tkd, 16)
            s.dma_start(out=w_qkf[:].rearrange("p (k m) -> p k m", k=KD),
                        in_=wqkT.rearrange("(k p) m -> p k m", p=128)
                        ).then_inc(tk.tkd, 16)
            return None
        STEP("sync", f, dma=True, inc=16 * (HPC + 2))

        def f(v):
            v.memset(ones_colb[:], 1.0)
            v.memset(ones_colf[:], 1.0)
            v.tensor_single_scalar(eye128f[:], iota_row[:],
                                   iota_col[:, 0:1], ALU.is_equal)
            return v.tensor_single_scalar(eye128[:], iota_row[:],
                                          iota_col[:, 0:1], ALU.is_equal)
        STEP("vector", f)

        def f(t):
            last = None
            for j, (w, oc) in enumerate(((w_qkf, ones_colf), (w_v, ones_colb))):
                for kk in range(KD):
                    last = t.matmul(pE[0:1, j * HPC * DH:(j + 1) * HPC * DH],
                                    oc[:, 0:1],
                                    w[:, kk * HPC * DH:(kk + 1) * HPC * DH],
                                    start=(kk == 0), stop=(kk == KD - 1))
            return last
        STEP("tensor", f)

        def f(v):
            return v.tensor_scalar_mul(nwsum[:], pE[0:1, 0:2 * HPC * DH], -1.0)
        STEP("vector", f)

        qk_fm = [SBP(f"qk_fm{b}", [128, S], F32) for b in range(B)]
        v_fm = [SBP(f"v_fm{b}", [128, S], BF16) for b in range(B)]
        es1 = contextlib.ExitStack()
        xmac = es1.enter_context(nc.sbuf_tensor("xmac", [128, KD * MAC], BF16))
        x2mac = es1.enter_context(nc.sbuf_tensor("x2mac", [128, KD * MAC], BF16))
        xf32 = es1.enter_context(nc.sbuf_tensor("xf32", [128, KD * MAC], F32))

        MPB = MAC // 128
        for m in range(NMAC):
            bat = m // (NMAC // B)
            toks = slice(m * MAC, (m + 1) * MAC)
            loc = slice((m % (NMAC // B)) * MAC, (m % (NMAC // B) + 1) * MAC)

            def f(g, toks=toks):
                g.dma_start(
                    out=xmac[:].rearrange("p (k t) -> p k t", k=KD),
                    in_=xf.rearrange("(k p) t -> p k t", p=128)[:, :, toks]
                ).then_inc(tk.tkg, 16)
                return None
            STEP("gpsimd", f, dma=True)

            def f(s, toks=toks):
                s.dma_start(
                    out=xf32[:].rearrange("p (k t) -> p k t", k=KD),
                    in_=xf.rearrange("(k p) t -> p k t", p=128)[:, :, toks]
                ).then_inc(tk.tkd, 16)
                return None
            STEP("sync", f, dma=True, inc=16)

            def f(v):
                return v.tensor_mul(x2mac[:], xmac[:], xmac[:])
            STEP("vector", f)

            def f(t):
                last = None
                for kk in range(KD):
                    last = t.matmul(pE[0:1, 0:MAC], ones_colb[:, 0:1],
                                    xmac[:, kk * MAC:(kk + 1) * MAC],
                                    start=(kk == 0), stop=(kk == KD - 1))
                for gblk in range(MPB):
                    ts_ = slice(gblk * 128, (gblk + 1) * 128)
                    for j, src in enumerate((xmac, x2mac)):
                        for kk in range(KD):
                            last = t.matmul(
                                pF[:, j * MPB + gblk:j * MPB + gblk + 1],
                                src[:, kk * MAC:(kk + 1) * MAC][:, ts_],
                                ones_colb[:, 0:1],
                                start=(kk == 0), stop=(kk == KD - 1))
                return last
            STEP("tensor", f)

            def f(v, m=m, toks=toks):
                v.tensor_scalar_mul(mu_row[0:1, 0:MAC], pE[0:1, 0:MAC], 1.0 / D)
                cs = slice(m * MPB, (m + 1) * MPB)
                v.tensor_copy(muc[:, cs], pF[:, 0:MPB])
                return v.tensor_copy(sx2c[:, cs], pF[:, MPB:2 * MPB])
            STEP("vector", f)

            def f(t, toks=toks):
                last = None
                for w, xx, dst, off in ((w_qkf, xf32, pA, 0),
                                        (w_v, xmac, pB, HPC * DH)):
                    for hh in range(HPC):
                        tp = (0, 64) if hh == 1 else None
                        outsl = dst[64 * hh:64 * hh + 64, 0:MAC]
                        for kk in range(KD):
                            t.matmul(outsl,
                                     w[:, kk * HPC * DH + hh * DH:
                                       kk * HPC * DH + (hh + 1) * DH],
                                     xx[:, kk * MAC:(kk + 1) * MAC],
                                     start=(kk == 0), stop=False,
                                     tile_position=tp)
                        last = t.matmul(
                            outsl,
                            nwsum[0:1, off + hh * DH:off + (hh + 1) * DH],
                            mu_row[0:1, 0:MAC],
                            start=False, stop=True, tile_position=tp)
                return last
            STEP("tensor", f)

            def f(v, bat=bat, loc=loc):
                v.tensor_copy(qk_fm[bat][:, loc], pA[:, 0:MAC])
                return v.tensor_copy(v_fm[bat][:, loc], pB[:, 0:MAC])
            STEP("vector", f)

        def f(v):
            v.tensor_scalar_mul(vscr[:], muc[:], 1.0 / D)
            v.tensor_mul(vscr2[:], vscr[:], vscr[:])
            v.scalar_tensor_tensor(vscr[:], sx2c[:], 1.0 / D, vscr2[:],
                                   ALU.mult, ALU.subtract)
            v.tensor_scalar_add(vscr2[:], vscr[:], 1.0e-5)
            return v.reciprocal(vscr[:], vscr2[:])
        STEP("vector", f)

        def f(s):
            return s.activation(r_col[:], vscr[:], AF.Sqrt)
        STEP("scalar", f)

        if "stats" in debug:
            def f(s):
                s.dma_start(out=dbg["dbg_stats"][0], in_=muc[:]).then_inc(tk.tkd, 16)
                s.dma_start(out=dbg["dbg_stats"][1], in_=r_col[:]).then_inc(tk.tkd, 16)
                return None
            STEP("sync", f, dma=True, inc=32)
        if "qkfm" in debug:
            def f(s):
                for b in range(B):
                    s.dma_start(out=dbg["dbg_qkfm"][b], in_=qk_fm[b][:]
                                ).then_inc(tk.tkd, 16)
                return None
            STEP("sync", f, dma=True, inc=16 * B)

        es1.close()

        es2 = contextlib.ExitStack()
        tabstage = es2.enter_context(
            nc.sbuf_tensor("tabstage", [128, NBLK * ROW], BF16))

        for u in range(UPC):
            bat, hh = u // HPC, u % HPC
            rs = slice(64 * hh, 64 * hh + 64)

            def f(v):
                return v.memset(tabstage[:], 0.0)
            STEP("vector", f)

            for g0 in range(0, NBLK, 2):
                def f(t, bat=bat, rs=rs, g0=g0):
                    last = None
                    for gg in range(2):
                        gblk = g0 + gg
                        cs = slice(gblk * 128, (gblk + 1) * 128)
                        t.transpose(pH[:, gg * 64:gg * 64 + 64],
                                    qk_fm[bat][rs, cs], eye128f[rs, rs])
                        last = t.transpose(pG[:, gg * 64:gg * 64 + 64],
                                           v_fm[bat][rs, cs], eye128[rs, rs])
                    return last
                STEP("tensor", f)

                def f(v, bat=bat, g0=g0):
                    last = None
                    for gg in range(2):
                        gblk = g0 + gg
                        rcc = bat * NBLK + gblk
                        v.tensor_single_scalar(
                            tabstage[:, gblk * ROW:gblk * ROW + 64],
                            pH[:, gg * 64:gg * 64 + 64],
                            r_col[:, rcc:rcc + 1], ALU.mult)
                        v.tensor_single_scalar(
                            tabstage[:, gblk * ROW + 128:gblk * ROW + 192],
                            pG[:, gg * 64:gg * 64 + 64],
                            r_col[:, rcc:rcc + 1], ALU.mult)
                        v.tensor_mul(escr[:, 0:64],
                                     tabstage[:, gblk * ROW:gblk * ROW + 64],
                                     tabstage[:, gblk * ROW:gblk * ROW + 64])
                        last = v.tensor_reduce(ssqu[:, gblk:gblk + 1],
                                               escr[:, 0:64], axis=AX.X,
                                               op=ALU.add)
                    return last
                STEP("vector", f)

            def f(v):
                v.tensor_scalar(vscr[:, 0:NBLK], ssqu[:], 1.0 / DH, 1.0e-6,
                                ALU.mult, ALU.add)
                return v.reciprocal(vscr2[:, 0:NBLK], vscr[:, 0:NBLK])
            STEP("vector", f)

            def f(s):
                return s.activation(scolb[:], vscr2[:, 0:NBLK], AF.Sqrt)
            STEP("scalar", f)

            def f(v):
                last = v.tensor_scalar_mul(scolb[:], scolb[:], 1.0 / 8.0)
                for gblk in range(NBLK):
                    last = v.tensor_single_scalar(
                        tabstage[:, gblk * ROW + 64:gblk * ROW + 128],
                        tabstage[:, gblk * ROW:gblk * ROW + 64],
                        scolb[:, gblk:gblk + 1], ALU.mult)
                for gblk in range(NBLK):
                    last = v.memset(
                        tabstage[:, gblk * ROW + 192:gblk * ROW + 193], 1.0)
                return last
            STEP("vector", f)

            def f(s, u=u):
                s.dma_start(out=tab[u].rearrange("(g p) r -> p g r", p=128),
                            in_=tabstage[:].rearrange("p (g r) -> p g r", r=ROW)
                            ).then_inc(tk.tkd, 16)
                return None
            STEP("sync", f, dma=True)

        if "tab" in debug:
            def f(s):
                for u in range(UPC):
                    s.dma_start(out=dbg["dbg_tab"][u], in_=tab[u]
                                ).then_inc(tk.tkd, 16)
                return None
            STEP("sync", f, dma=True, inc=16 * UPC)

        for u in range(UPC):
            bat, hh = u // HPC, u % HPC
            rs = slice(64 * hh, 64 * hh + 64)
            for bk in range(NBLK):
                cs = slice(bk * 128, (bk + 1) * 128)
                banks = (pA, pB)

                def f(t, bat=bat, rs=rs, cs=cs, hh=hh):
                    last = None
                    for kb in range(NBK):
                        last = t.matmul(banks[kb][:, 0:RVW], qk_fm[bat][rs, cs],
                                        rot_s[hh][rs, kb * RVW:(kb + 1) * RVW],
                                        start=True, stop=True)
                    return last
                STEP("tensor", f)

                def f(v):
                    nseg = NHASH // NBK
                    last = None
                    for kb in range(NBK):
                        last = v.tensor_reduce(
                            mabs[:, kb * nseg:(kb + 1) * nseg],
                            banks[kb][:, 0:RVW].rearrange("p (n j) -> p n j",
                                                          n=nseg),
                            axis=AX.X, op=ALU.max, apply_absolute_value=True)
                    return last
                STEP("vector", f)

                def f(v):
                    return v.tensor_scalar_mul(nbias[:], mabs[:], -1.0)
                STEP("vector", f)

                def f(v):
                    nseg = NHASH // NBK
                    last = None
                    for kb in range(NBK):
                        for n2 in range(nseg):
                            n = kb * nseg + n2
                            src = banks[kb][:, n2 * NB:(n2 + 1) * NB]
                            v.tensor_single_scalar(
                                escr[:, n * NB:(n + 1) * NB], src,
                                mabs[:, n:n + 1], ALU.is_equal)
                            last = v.tensor_single_scalar(
                                escr[:, JW + n * NB:JW + (n + 1) * NB], src,
                                nbias[:, n:n + 1], ALU.is_equal)
                    return last
                STEP("vector", f)

                def f(v):
                    v.tensor_mul(pscr[:, 0:JW], escr[:, 0:JW], ctile[:])
                    return v.tensor_mul(pscr[:, JW:2 * JW], escr[:, JW:2 * JW],
                                        ctile[:])
                STEP("vector", f)

                def f(v):
                    v.tensor_reduce(accP[:],
                                    pscr[:, 0:JW].rearrange("p (n j) -> p n j",
                                                            n=NHASH),
                                    axis=AX.X, op=ALU.max)
                    return v.tensor_reduce(
                        accN[:],
                        pscr[:, JW:2 * JW].rearrange("p (n j) -> p n j", n=NHASH),
                        axis=AX.X, op=ALU.max)
                STEP("vector", f)

                def f(v, u=u, bk=bk):
                    v.tensor_scalar(rndP[:], accP[:], RND_C, RND_C, ALU.add,
                                    ALU.subtract)
                    return v.tensor_scalar(rndN[:], accN[:], RND_C, RND_C,
                                           ALU.add, ALU.subtract)
                STEP("vector", f)

                def f(v, u=u, bk=bk):
                    v.tensor_max(msel[:], rndP[:], rndN[:])
                    return v.tensor_tensor(negf[:], rndN[:], rndP[:], ALU.is_le)
                STEP("vector", f)

                def f(v, u=u, bk=bk):
                    return v.scalar_tensor_tensor(
                        kystg[:], negf[:], -float(NB * S),
                        t2til[:, bk * NHASH:(bk + 1) * NHASH], ALU.mult, ALU.add)
                STEP("vector", f)

                def f(v, u=u, bk=bk):
                    return v.scalar_tensor_tensor(
                        keysA[:, 0:FB].rearrange("p (uu n k) -> p uu n k", uu=UPC,
                                                 n=NHASH)[:, u, :, bk],
                        msel[:], -float(S), kystg[:], ALU.mult, ALU.add)
                STEP("vector", f)

                if "ph3" in debug and u == 0 and bk == 33:
                    def f(s):
                        s.dma_start(out=dbg["dbg_p3escr"], in_=escr[:]
                                    ).then_inc(tk.tkd, 16)
                        s.dma_start(out=dbg["dbg_p3pscr"], in_=pscr[:]
                                    ).then_inc(tk.tkd, 16)
                        s.dma_start(out=dbg["dbg_p3sm"],
                                    in_=mabs[:]).then_inc(tk.tkd, 16)
                        s.dma_start(out=dbg["dbg_p3sm2"],
                                    in_=accP[:]).then_inc(tk.tkd, 16)
                        s.dma_start(out=dbg["dbg_p3sm3"],
                                    in_=accN[:]).then_inc(tk.tkd, 16)
                        s.dma_start(out=dbg["dbg_p3sm4"],
                                    in_=msel[:]).then_inc(tk.tkd, 16)
                        s.dma_start(out=dbg["dbg_p3sm5"],
                                    in_=negf[:]).then_inc(tk.tkd, 16)
                        s.dma_start(out=dbg["dbg_p3t2"],
                                    in_=t2til[:]).then_inc(tk.tkd, 16)
                        return None
                    STEP("sync", f, dma=True, inc=128)

        if "keys" in debug:
            def f(s):
                s.dma_start(out=dbg["dbg_keys"], in_=keysA[:, 0:FB]).then_inc(tk.tkd, 16)
                return None
            STEP("sync", f, dma=True)

        assert GB <= 64
        cur = [keysA, keysB]
        par = [0]

        def emit_flip(to_f):
            src = cur[par[0]]
            dst = cur[1 - par[0]]
            for sl in range(NSL):
                def f(t, sl=sl, src=src, to_f=to_f):
                    if to_f:
                        return t.transpose(pH[0:GB, 0:128],
                                           src[:, sl * GB:(sl + 1) * GB],
                                           eye128f[:])
                    else:
                        return t.transpose(pH[0:128, 0:GB],
                                           src[0:GB, sl * 128:(sl + 1) * 128],
                                           eye128f[0:GB, 0:GB])
                STEP("tensor", f)

                def f(v, sl=sl, dst=dst, to_f=to_f):
                    if to_f:
                        return v.tensor_copy(dst[0:GB, sl * 128:(sl + 1) * 128],
                                             pH[0:GB, 0:128])
                    else:
                        return v.tensor_copy(dst[:, sl * GB:(sl + 1) * GB],
                                             pH[:, 0:GB])
                STEP("vector", f)
            par[0] = 1 - par[0]

        def emit_sign(p):
            src = cur[par[0]]

            def f(v, p=p, src=src):
                last = None
                for sl in range(NSL):
                    last = v.tensor_mul(
                        src[0:GB, sl * 128:(sl + 1) * 128],
                        src[0:GB, sl * 128:(sl + 1) * 128],
                        sgf_s[0:GB, (p - 1) * 128:p * 128])
                return last
            STEP("vector", f)

        def emit_cmpx(b, in_f):
            src = cur[par[0]]
            dst = cur[1 - par[0]]

            def f(v, b=b, in_f=in_f, src=src, dst=dst):
                if in_f:
                    span = 1 << b
                    nhi = 128 // (span * 2)
                    sv = src[0:GB, 0:NSL * 128].rearrange(
                        "p (s h two l) -> p s h two l", s=NSL, h=nhi, two=2)
                    dvv = dst[0:GB, 0:NSL * 128].rearrange(
                        "p (s h two l) -> p s h two l", s=NSL, h=nhi, two=2)
                else:
                    span = 1 << (b - 7)
                    nhi = GB // (span * 2)
                    sv = src[:, 0:FB].rearrange(
                        "p (s h two l) -> p s h two l", s=NSL, h=nhi, two=2)
                    dvv = dst[:, 0:FB].rearrange(
                        "p (s h two l) -> p s h two l", s=NSL, h=nhi, two=2)
                A = sv[:, :, :, 0]
                Bv = sv[:, :, :, 1]
                v.tensor_tensor(dvv[:, :, :, 0], A, Bv, ALU.min)
                return v.tensor_tensor(dvv[:, :, :, 1], A, Bv, ALU.max)
            STEP("vector", f)
            par[0] = 1 - par[0]

        fbits = 7
        emit_flip(True)
        in_f = True
        sched = bitonic_schedule(nbits)
        if sort_trunc is not None:
            sched = sched[:sort_trunc]
        for op, a in sched:
            if op == "sign":
                assert in_f
                emit_sign(a)
            else:
                want_f = a < fbits
                if want_f != in_f:
                    emit_flip(want_f)
                    in_f = want_f
                emit_cmpx(a, in_f)
        if in_f:
            emit_flip(False)

        if "skeys" in debug:
            def f(s):
                s.dma_start(out=dbg["dbg_skeys"], in_=cur[par[0]][:, 0:FB]
                            ).then_inc(tk.tkd, 16)
                return None
            STEP("sync", f, dma=True)

        def f(v):
            v.tensor_copy(sidx[:], cur[par[0]][:, 0:FB])
            return v.tensor_single_scalar(sidx[:], sidx[:], S - 1,
                                          ALU.bitwise_and)
        STEP("vector", f)

        if "sidx" in debug:
            def f(s):
                s.dma_start(out=dbg["dbg_sidx"], in_=sidx[:]).then_inc(tk.tkd, 16)
                return None
            STEP("sync", f, dma=True)

        if stop_after_sort:
            wc, wd, wg = tk.barrier_values()

            def fbar(s):
                s.wait_ge(tk.tkc, wc)
                s.wait_ge(tk.tkd, wd)
                s.wait_ge(tk.tkg, wg)
                return None
            tk.steps["sync"].append(lambda engine: fbar(engine))
            with nc.Block() as block:
                tk.run(block)
            es2.close()
            esP.close()
            return nc, dbg
        es2.close()
        esP.close()
        es3 = contextlib.ExitStack()
        SB3 = lambda *a: es3.enter_context(nc.sbuf_tensor(*a))
        gtile = SB3("gtile", [128, 16 * ROW], BF16)
        qsf = SB3("qsf", [64, S], BF16)
        ksf = SB3("ksf", [64, S], BF16)
        vsf = SB3("vsf", [128, NT * 65], BF16)
        vsf2 = SB3("vsf2", [128, NT * 65], BF16)
        expb = SB3("expb", [128, 2 * 512 + 64], BF16)
        scst = SB3("scst", [128, NBLK * 66], F32)
        comb = SB3("comb", [128, NBLK * 66], F32)
        comb2 = SB3("comb2", [128, NBLK * 66], F32)
        zrec = SB3("zrec", [128, NBLK], F32)
        hstg = SB3("hstg", [128, NBLK * DH], F32)
        qhd = SB3("qhd", [64, NHASH * 64], BF16)
        khd = SB3("khd", [64, NHASH * 64], BF16)
        ktl = SB3("ktl", [64, NHASH * 64], BF16)
        vhd = SB3("vhd", [128, NHASH * 65], BF16)
        vtl = SB3("vtl", [128, NHASH * 65], BF16)
        vcat = SB3("vcat", [128, 65], BF16)
        c1sav = SB3("c1sav", [64, 66], F32)

        def f(v):
            return v.memset(scst[:], 0.0)
        STEP("vector", f)

        def emit_chunk0(u, n, np_):
            def f(gq, n=n, np_=np_):
                gq.dma_start(out=vcat[0:64, :],
                             in_=vtl[64:128, np_ * 65:(np_ + 1) * 65]
                             ).then_inc(tk.tkg, 16)
                gq.dma_start(out=vcat[64:128, :],
                             in_=vhd[0:64, n * 65:(n + 1) * 65]
                             ).then_inc(tk.tkg, 16)
                return None
            STEP("gpsimd", f, dma=True, inc=32)

            def f(t, n=n, np_=np_):
                t.matmul(pA[0:64, 448:512],
                         ktl[:, np_ * 64:(np_ + 1) * 64],
                         qhd[:, n * 64:(n + 1) * 64],
                         start=True, stop=True)
                return t.matmul(pA[64:128, 448:512],
                                khd[:, n * 64:(n + 1) * 64],
                                qhd[:, n * 64:(n + 1) * 64],
                                start=True, stop=True, tile_position=(0, 64))
            STEP("tensor", f)

            def f(s):
                return s.activation(expb[:, 1024:1024 + 64], pA[:, 448:512],
                                    AF.Exp)
            STEP("scalar", f)

            def f(t):
                return t.matmul(pC[0:64, 396:461], expb[:, 1024:1024 + 64],
                                vcat[:], start=True, stop=True)
            STEP("tensor", f)

            def f(v):
                return v.tensor_copy(scst[0:64, 0:65], pC[0:64, 396:461])
            STEP("vector", f)

        def emit_scatter(u, n, sl, gstart, gend):
            def f(gq, u=u, n=n, sl=sl, gstart=gstart, gend=gend):
                for g in range(gstart, gend):
                    gq.indirect_dma_start(
                        Obuf[u][n],
                        IndirectOffsetOnAxis(
                            sidx[:, sl * GB + g:sl * GB + g + 1], 0),
                        scst[:, g * 66:(g + 1) * 66], None,
                    ).then_inc(tk.tkg, 16)
                return None
            STEP("gpsimd", f, dma=True, inc=16 * (gend - gstart))

        for u in range(UPC):
            for n in range(NHASH):
                sl = u * NHASH + n

                for g0 in range(0, NT, 4):
                    def f(gq, u=u, sl=sl, g0=g0):
                        for g in range(g0, min(g0 + 4, NT)):
                            gq.indirect_dma_start(
                                gtile[:, (g % 16) * ROW:(g % 16 + 1) * ROW],
                                None, tab[u],
                                IndirectOffsetOnAxis(
                                    sidx[:, sl * GB + g:sl * GB + g + 1], 0),
                            ).then_inc(tk.tkg, 16)
                        return None
                    STEP("gpsimd", f, dma=True,
                         inc=16 * (min(g0 + 4, NT) - g0))

                    def f(t, g0=g0):
                        last = None
                        for g in range(g0, min(g0 + 4, NT)):
                            so = (g % 16) * ROW
                            co = (g % 4) * 256
                            t.transpose(pG[0:64, co:co + 128],
                                        gtile[:, so:so + 64], eye128[:])
                            last = t.transpose(pG[0:64, co + 128:co + 256],
                                               gtile[:, so + 64:so + 128],
                                               eye128[:])
                        return last
                    STEP("tensor", f)

                    def f(v, g0=g0):
                        last = None
                        for g in range(g0, min(g0 + 4, NT)):
                            so = (g % 16) * ROW
                            co = (g % 4) * 256
                            v.tensor_copy(qsf[:, g * 128:(g + 1) * 128],
                                          pG[0:64, co:co + 128])
                            v.tensor_copy(ksf[:, g * 128:(g + 1) * 128],
                                          pG[0:64, co + 128:co + 256])
                            last = v.tensor_copy(
                                vsf[:, g * 65:(g + 1) * 65],
                                gtile[:, so + 128:so + 193])
                        return last
                    STEP("vector", f)

                def f(v, n=n):
                    v.tensor_copy(qhd[:, n * 64:(n + 1) * 64], qsf[:, 0:64])
                    v.tensor_copy(khd[:, n * 64:(n + 1) * 64], ksf[:, 0:64])
                    v.tensor_copy(ktl[:, n * 64:(n + 1) * 64],
                                  ksf[:, (NCH - 1) * 64:NCH * 64])
                    v.tensor_copy(vhd[:, n * 65:(n + 1) * 65], vsf[:, 0:65])
                    return v.tensor_copy(vtl[:, n * 65:(n + 1) * 65],
                                         vsf[:, (NT - 1) * 65:NT * 65])
                STEP("vector", f)

                def f(gq):
                    gq.dma_start(out=vsf2[0:64, :], in_=vsf[64:128, :]
                                 ).then_inc(tk.tkg, 16)
                    gq.dma_start(
                        out=vsf2[64:128, 0:(NT - 1) * 65],
                        in_=vsf[0:64, 65:NT * 65]).then_inc(tk.tkg, 16)
                    return None
                STEP("gpsimd", f, dma=True, inc=32)

                for c0 in range(0, NCH, 8):
                    cs0 = max(c0, 1)
                    bank = pA if (c0 // 8) % 2 == 0 else pB

                    def f(t, cs0=cs0, c0=c0, bank=bank):
                        last = None
                        for c in range(cs0, c0 + 8):
                            last = t.matmul(bank[:, (c % 8) * 64:(c % 8 + 1) * 64],
                                            ksf[:, (c - 1) * 64:(c + 1) * 64],
                                            qsf[:, c * 64:(c + 1) * 64],
                                            start=True, stop=True)
                        return last
                    STEP("tensor", f)

                    def f(s, cs0=cs0, c0=c0, bank=bank):
                        co = (cs0 % 8) * 64
                        return s.activation(
                            expb[:, (c0 // 8) % 2 * 512 + co:
                                 (c0 // 8) % 2 * 512 + 512],
                            bank[:, co:512], AF.Exp)
                    STEP("scalar", f)

                    def f(t, cs0=cs0, c0=c0):
                        last = None
                        for c in range(cs0, c0 + 8):
                            pvb = pC if (c // 4) % 2 == 0 else pD
                            pvreg = ((c % 4) // 2) * 66
                            pvhalf = (c % 2) * 64
                            tp = (0, 64) if pvhalf else None
                            ecol = (c0 // 8) % 2 * 512 + (c % 8) * 64
                            if c % 2 == 1:
                                rhs = vsf[:, ((c - 1) // 2) * 65:
                                          ((c - 1) // 2 + 1) * 65]
                            else:
                                rhs = vsf2[:, ((c - 2) // 2) * 65:
                                           ((c - 2) // 2 + 1) * 65]
                            last = t.matmul(
                                pvb[pvhalf:pvhalf + 64, pvreg:pvreg + 65],
                                expb[:, ecol:ecol + 64], rhs,
                                start=True, stop=True, tile_position=tp)
                        return last
                    STEP("tensor", f)

                    def f(v, cs0=cs0, c0=c0):
                        last = None
                        for c in range(cs0, c0 + 8):
                            pvb = pC if (c // 4) % 2 == 0 else pD
                            pvreg = ((c % 4) // 2) * 66
                            half = (c % 2) * 64
                            blk = c // 2
                            last = v.tensor_copy(
                                scst[half:half + 64, blk * 66:blk * 66 + 65],
                                pvb[half:half + 64, pvreg:pvreg + 65])
                        return last
                    STEP("vector", f)

                if n > 0:
                    emit_chunk0(u, n, n - 1)
                    emit_scatter(u, n, sl, 0, NT)
                else:
                    def f(v):
                        return v.tensor_copy(c1sav[:], scst[64:128, 0:66])
                    STEP("vector", f)
                    emit_scatter(u, n, sl, 1, NT)

            emit_chunk0(u, 0, NHASH - 1)

            def f(v):
                return v.tensor_copy(scst[64:128, 0:66], c1sav[:])
            STEP("vector", f)
            emit_scatter(u, 0, u * NHASH, 0, 1)

            if "obuf" in debug:
                def f(s, u=u):
                    for n in range(NHASH):
                        s.dma_start(out=dbg["dbg_obuf"][u, n], in_=Obuf[u][n]
                                    ).then_inc(tk.tkd, 16)
                    return None
                STEP("sync", f, dma=True, inc=16 * NHASH)

            def f(s, u=u):
                s.dma_start(out=comb[:].rearrange("p (g r) -> p g r", r=66),
                            in_=Obuf[u][0].rearrange("(g p) r -> p g r", p=128)
                            ).then_inc(tk.tkd, 16)
                return None
            STEP("sync", f, dma=True, inc=16)
            for n in range(1, NHASH):
                def f(s, u=u, n=n):
                    s.dma_start(
                        out=comb2[:].rearrange("p (g r) -> p g r", r=66),
                        in_=Obuf[u][n].rearrange("(g p) r -> p g r", p=128)
                    ).then_inc(tk.tkd, 16)
                    return None
                STEP("sync", f, dma=True, inc=16)

                def f(v):
                    return v.tensor_add(comb[:], comb[:], comb2[:])
                STEP("vector", f)

            def f(v):
                last = None
                for g in range(NBLK):
                    last = v.reciprocal(zrec[:, g:g + 1],
                                        comb[:, g * 66 + 64:g * 66 + 65])
                return last
            STEP("vector", f)

            if "comb" in debug and u == 1:
                def f(s):
                    s.dma_start(out=dbg["dbg_comb"], in_=comb[:]
                                ).then_inc(tk.tkd, 16)
                    s.dma_start(out=dbg["dbg_zrec"], in_=zrec[:]
                                ).then_inc(tk.tkd, 16)
                    return None
                STEP("sync", f, dma=True, inc=32)

            def f(s):
                last = None
                for g in range(NBLK):
                    last = s.activation(hstg[:, g * DH:(g + 1) * DH],
                                        comb[:, g * 66:g * 66 + 64], AF.Copy,
                                        scale=zrec[:, g:g + 1])
                return last
            STEP("scalar", f)

            def f(s, u=u):
                s.dma_start(out=H_out[u].rearrange("(g p) d -> p g d", p=128),
                            in_=hstg[:].rearrange("p (g d) -> p g d", d=DH)
                            ).then_inc(tk.tkd, 16)
                return None
            STEP("sync", f, dma=True, inc=16)

        wc, wd, wg = tk.barrier_values()

        def fbar(s):
            s.wait_ge(tk.tkc, wc)
            s.wait_ge(tk.tkd, wd)
            s.wait_ge(tk.tkg, wg)
            return None
        tk.steps["sync"].append(lambda engine: fbar(engine))

        with nc.Block() as block:
            tk.run(block)
        es3.close()

    return nc, dbg




def emit_bitonic(tk, STEP, nc, bufs, par, pXf, eyef, sgf_s, NSL, GB, nbits,
                 fbits=7):
    FB = NSL * GB
    cur = bufs

    def emit_flip(to_f):
        src = cur[par[0]]
        dst = cur[1 - par[0]]
        for sl in range(NSL):
            def f(t, sl=sl, src=src, to_f=to_f):
                if to_f:
                    return t.transpose(pXf[0:GB, 0:128],
                                       src[:, sl * GB:(sl + 1) * GB], eyef[:])
                else:
                    return t.transpose(pXf[0:128, 0:GB],
                                       src[0:GB, sl * 128:(sl + 1) * 128],
                                       eyef[0:GB, 0:GB])
            STEP("tensor", f)

            def f(v, sl=sl, dst=dst, to_f=to_f):
                if to_f:
                    return v.tensor_copy(dst[0:GB, sl * 128:(sl + 1) * 128],
                                         pXf[0:GB, 0:128])
                else:
                    return v.tensor_copy(dst[:, sl * GB:(sl + 1) * GB],
                                         pXf[:, 0:GB])
            STEP("vector", f)
        par[0] = 1 - par[0]

    def emit_sign(p):
        src = cur[par[0]]

        def f(v, p=p, src=src):
            last = None
            for sl in range(NSL):
                last = v.tensor_mul(
                    src[0:GB, sl * 128:(sl + 1) * 128],
                    src[0:GB, sl * 128:(sl + 1) * 128],
                    sgf_s[0:GB, (p - 1) * 128:p * 128])
            return last
        STEP("vector", f)

    def emit_cmpx(b, in_f):
        src = cur[par[0]]
        dst = cur[1 - par[0]]

        def f(v, b=b, in_f=in_f, src=src, dst=dst):
            if in_f:
                span = 1 << b
                nhi = 128 // (span * 2)
                sv = src[0:GB, 0:NSL * 128].rearrange(
                    "p (s h two l) -> p s h two l", s=NSL, h=nhi, two=2)
                dvv = dst[0:GB, 0:NSL * 128].rearrange(
                    "p (s h two l) -> p s h two l", s=NSL, h=nhi, two=2)
            else:
                span = 1 << (b - fbits)
                nhi = GB // (span * 2)
                sv = src[:, 0:FB].rearrange(
                    "p (s h two l) -> p s h two l", s=NSL, h=nhi, two=2)
                dvv = dst[:, 0:FB].rearrange(
                    "p (s h two l) -> p s h two l", s=NSL, h=nhi, two=2)
            A = sv[:, :, :, 0]
            Bv = sv[:, :, :, 1]
            v.tensor_tensor(dvv[:, :, :, 0], A, Bv, ALU.min)
            return v.tensor_tensor(dvv[:, :, :, 1], A, Bv, ALU.max)
        STEP("vector", f)
        par[0] = 1 - par[0]

    emit_flip(True)
    in_f = True
    for op, a in bitonic_schedule(nbits):
        if op == "sign":
            assert in_f
            emit_sign(a)
        else:
            want_f = a < fbits
            if want_f != in_f:
                emit_flip(want_f)
                in_f = want_f
            emit_cmpx(a, in_f)
    if in_f:
        emit_flip(False)



MOE_C = 768


def build_ffn(cfg: Cfg, debug=()):
    D, TPC = cfg.D, cfg.TPC
    E, I, C = 8, 2048, MOE_C
    KD = D // 128
    NTT = TPC // 128
    nbits2 = int(math.log2(TPC))
    GB2 = TPC // 128
    ROW2 = 1024
    KI = I // 128
    CG = C // 128
    K_I8 = 40000.0
    EMASK = ((1 << 24) - 1) ^ (TPC - 1)

    nc = bass.Bass("TRN2", target_bir_lowering=False, detect_race_conditions=False)

    xsl = nc.dram_tensor("xsl", [D, TPC], F32, kind="ExternalInput").ap()
    Hfm = nc.dram_tensor("Hfm", [D, TPC], F32, kind="ExternalInput").ap()
    woT = nc.dram_tensor("woT", [D, D], F32, kind="ExternalInput").ap()
    lnw_i = nc.dram_tensor("lnw", [D], F32, kind="ExternalInput").ap()
    lnb_i = nc.dram_tensor("lnb", [D], F32, kind="ExternalInput").ap()
    rw_i = nc.dram_tensor("rw", [D, E], F32, kind="ExternalInput").ap()
    gw_i = nc.dram_tensor("gw", [E, D, I], F32, kind="ExternalInput").ap()
    uw_i = nc.dram_tensor("uw", [E, D, I], F32, kind="ExternalInput").ap()
    dw_i = nc.dram_tensor("dw", [E, I, D], F32, kind="ExternalInput").ap()
    shg_i = nc.dram_tensor("shg", [D, I], F32, kind="ExternalInput").ap()
    shu_i = nc.dram_tensor("shu", [D, I], F32, kind="ExternalInput").ap()
    shd_i = nc.dram_tensor("shd", [I, D], F32, kind="ExternalInput").ap()
    sgf2 = nc.dram_tensor("sgf2", [nbits2 - 1, 128, 128], F32,
                          kind="ExternalInput").ap()
    gidx0 = nc.dram_tensor("gidx0", [E * C], I32, kind="ExternalInput").ap()
    out_o = nc.dram_tensor("out", [D, TPC], F32, kind="ExternalOutput").ap()

    xtab = nc.dram_tensor("xtab", [TPC, ROW2], BF16, kind="Internal").ap()
    gidx = nc.dram_tensor("gidx", [E * C], I32, kind="Internal").ap()
    moetmp = nc.dram_tensor("moetmp", [TPC, D], BF16, kind="Internal").ap()
    rb2 = nc.dram_tensor("rb2", [2 * TPC], F32, kind="Internal").ap()

    dbg = {}
    for name in debug:
        if name == "x0":
            dbg["dbg_x0"] = nc.dram_tensor("dbg_x0", [D, TPC], F32,
                                           kind="ExternalOutput").ap()
        elif name == "wo":
            dbg["dbg_wo"] = nc.dram_tensor("dbg_wo", [128, 8 * 1024], BF16,
                                           kind="ExternalOutput").ap()
        elif name == "hbf":
            dbg["dbg_hbf"] = nc.dram_tensor("dbg_hbf", [128, 8 * 2048], BF16,
                                            kind="ExternalOutput").ap()
        elif name == "x1":
            dbg["dbg_x1"] = nc.dram_tensor("dbg_x1", [D, TPC], F32,
                                           kind="ExternalOutput").ap()
        elif name == "ln2":
            dbg["dbg_ln2"] = nc.dram_tensor("dbg_ln2", [D, TPC], F32,
                                            kind="ExternalOutput").ap()
        elif name == "logits":
            dbg["dbg_logits"] = nc.dram_tensor("dbg_logits", [128, NTT * E],
                                               F32, kind="ExternalOutput").ap()
        elif name == "eid":
            dbg["dbg_eid"] = nc.dram_tensor("dbg_eid", [128, NTT], F32,
                                            kind="ExternalOutput").ap()
        elif name == "scor":
            dbg["dbg_scor"] = nc.dram_tensor("dbg_scor", [128, NTT], F32,
                                             kind="ExternalOutput").ap()
        elif name == "keys":
            dbg["dbg_keys"] = nc.dram_tensor("dbg_keys", [128, GB2], F32,
                                             kind="ExternalOutput").ap()
        elif name == "gidx":
            dbg["dbg_gidx"] = nc.dram_tensor("dbg_gidx", [E * C], I32,
                                             kind="ExternalOutput").ap()
        elif name == "moe":
            dbg["dbg_moe"] = nc.dram_tensor("dbg_moe", [TPC, D], BF16,
                                            kind="ExternalOutput").ap()

    import contextlib
    es = contextlib.ExitStack()
    SBUF = lambda *a: es.enter_context(nc.sbuf_tensor(*a))
    PSUM = lambda *a: es.enter_context(nc.psum_tensor(*a))

    with es:
        pA = PSUM("pA", [128, 512], F32)
        pB = PSUM("pB", [128, 512], F32)
        pC = PSUM("pC", [128, 512], F32)
        pD = PSUM("pD", [128, 512], F32)
        pE_ = PSUM("pE", [128, 512], F32)
        pF = PSUM("pF", [128, 128], F32)
        pG = PSUM("pG", [128, 512], BF16)
        pH = PSUM("pH", [128, 128], F32)

        A = SBUF("A", [128, KD * TPC], F32)
        ln2b = SBUF("ln2b", [128, KD * TPC], BF16)
        lnw_c = SBUF("lnw_c", [128, KD], F32)
        lnb_c = SBUF("lnb_c", [128, KD], F32)
        rwf = SBUF("rwf", [128, KD * E], F32)
        muc2 = SBUF("muc2", [128, NTT], F32)
        sxc2 = SBUF("sxc2", [128, NTT], F32)
        vs1 = SBUF("vs1", [128, NTT], F32)
        vs2 = SBUF("vs2", [128, NTT], F32)
        m8 = SBUF("m8", [128, NTT], F32)
        scor = SBUF("scor", [128, NTT], F32)
        eidf = SBUF("eidf", [128, NTT], F32)
        e8 = SBUF("e8", [128, NTT * E], BF16)
        p8 = SBUF("p8", [128, NTT * E], BF16)
        c8t = SBUF("c8t", [128, NTT * E], BF16)
        keysA = SBUF("keysA", [128, 128], F32)
        keysB = SBUF("keysB", [128, 128], F32)
        sgf_s2 = SBUF("sgf_s2", [128, (nbits2 - 1) * 128], F32)
        eids = SBUF("eids", [128, GB2], F32)
        tvals = SBUF("tvals", [128, GB2], I32)
        slots = SBUF("slots", [128, GB2], I32)
        slotsf = SBUF("slotsf", [128, GB2], F32)
        startv = SBUF("startv", [128, GB2], F32)
        t8til = SBUF("t8til", [128, GB2], F32)
        eqs = SBUF("eqs", [128, GB2 * E], F32)
        cnts = SBUF("cnts", [128, E], F32)
        startc = SBUF("startc", [128, E], F32)
        startbc = SBUF("startbc", [128, E], F32)
        ltri = SBUF("ltri", [128, E], F32)
        rstr = SBUF("rstr", [1, E], F32)
        iota_col = SBUF("iota_col", [128, 1], F32)
        iota_row = SBUF("iota_row", [128, 128], F32)
        ones_colf = SBUF("ones_colf", [128, 1], F32)
        ones_rowf = SBUF("ones_rowf", [1, 128], F32)
        eye128 = SBUF("eye128", [128, 128], BF16)
        eye128f = SBUF("eye128f", [128, 128], F32)
        idxcol = SBUF("idxcol", [128, E * CG], I32)

        tk = Ticked(nc, es)
        STEP = tk.step
        regs = {}

        def f(g):
            regs["bcT"] = g.alloc_register("bcT")
            g.reg_mov(regs["bcT"], TPC - 1)
            regs["bcG"] = g.alloc_register("bcG")
            g.reg_mov(regs["bcG"], E * C - 1)
            g.iota(iota_col[:], pattern=[[0, 1]], base=0, channel_multiplier=1,
                   allow_small_or_imprecise_dtypes=True)
            g.iota(iota_row[:], pattern=[[1, 128]], base=0, channel_multiplier=0,
                   allow_small_or_imprecise_dtypes=True)
            g.iota(c8t[:].rearrange("p (t e) -> p t e", t=NTT),
                   pattern=[[0, NTT], [-1, E]], base=E,
                   channel_multiplier=0, allow_small_or_imprecise_dtypes=True)
            return g.iota(t8til[:], pattern=[[128, GB2]], base=0,
                          channel_multiplier=1,
                          allow_small_or_imprecise_dtypes=True)
        STEP("gpsimd", f)

        def f(s):
            s.dma_start(out=A[:].rearrange("p (k t) -> p k t", k=KD),
                        in_=xsl.rearrange("(k p) t -> p k t", p=128)
                        ).then_inc(tk.tkd, 16)
            s.dma_start(out=lnw_c[:], in_=lnw_i.rearrange("(k p) -> p k", p=128),
                        allow_slow_non_contiguous=True).then_inc(tk.tkd, 16)
            s.dma_start(out=lnb_c[:], in_=lnb_i.rearrange("(k p) -> p k", p=128),
                        allow_slow_non_contiguous=True).then_inc(tk.tkd, 16)
            s.dma_start(out=rwf[:].rearrange("p (k e) -> p k e", k=KD),
                        in_=rw_i.rearrange("(k p) e -> p k e", p=128)
                        ).then_inc(tk.tkd, 16)
            s.dma_start(out=sgf_s2[:].rearrange("p (a b) -> p a b", a=nbits2 - 1),
                        in_=sgf2.rearrange("a p b -> p a b")).then_inc(tk.tkd, 16)
            s.dma_start(out=gidx, in_=gidx0).then_inc(tk.tkd, 16)
            return None
        STEP("sync", f, dma=True, inc=96)

        def f(v):
            v.memset(ones_colf[:], 1.0)
            v.memset(ones_rowf[:], 1.0)
            v.tensor_single_scalar(eye128f[:], iota_row[:], iota_col[:, 0:1],
                                   ALU.is_equal)
            return v.tensor_single_scalar(eye128[:], iota_row[:],
                                          iota_col[:, 0:1], ALU.is_equal)
        STEP("vector", f)

        es0 = contextlib.ExitStack()
        wo_b = es0.enter_context(nc.sbuf_tensor("wo_b", [128, KD * D], BF16))
        h_bf = es0.enter_context(nc.sbuf_tensor("h_bf", [128, KD * TPC], BF16))

        def f(g):
            g.dma_start(out=wo_b[:].rearrange("p (k m) -> p k m", k=KD),
                        in_=woT.rearrange("(k p) m -> p k m", p=128)
                        ).then_inc(tk.tkg, 16)
            g.dma_start(out=h_bf[:].rearrange("p (k t) -> p k t", k=KD),
                        in_=Hfm.rearrange("(k p) t -> p k t", p=128)
                        ).then_inc(tk.tkg, 16)
            return None
        STEP("gpsimd", f, dma=True, inc=32)

        if "x0" in debug:
            def f(s):
                s.dma_start(out=dbg["dbg_x0"].rearrange("(k p) t -> p k t", p=128),
                            in_=A[:].rearrange("p (k t) -> p k t", k=KD)
                            ).then_inc(tk.tkd, 16)
                return None
            STEP("sync", f, dma=True, inc=16)
        if "wo" in debug:
            def f(s):
                s.dma_start(out=dbg["dbg_wo"], in_=wo_b[:]).then_inc(tk.tkd, 16)
                return None
            STEP("sync", f, dma=True, inc=16)
        if "hbf" in debug:
            def f(s):
                s.dma_start(out=dbg["dbg_hbf"], in_=h_bf[:]).then_inc(tk.tkd, 16)
                return None
            STEP("sync", f, dma=True, inc=16)

        for mt in range(NTT // 4):
            ts_ = slice(mt * 512, (mt + 1) * 512)
            for mchunk in range(KD):
                pq = (pA, pB, pC, pD)[mchunk % 4]

                def f(t, mchunk=mchunk, ts_=ts_, pq=pq):
                    last = None
                    for kk in range(KD):
                        last = t.matmul(
                            pq[:, 0:512],
                            wo_b[:, kk * D + mchunk * 128:
                                 kk * D + (mchunk + 1) * 128],
                            h_bf[:, kk * TPC:(kk + 1) * TPC][:, ts_],
                            start=(kk == 0), stop=(kk == KD - 1))
                    return last
                STEP("tensor", f)

                def f(v, mchunk=mchunk, ts_=ts_, pq=pq):
                    asl = A[:, mchunk * TPC:(mchunk + 1) * TPC][:, ts_]
                    return v.tensor_add(asl, asl, pq[:, 0:512])
                STEP("vector", f)
        es0.close()

        if "x1" in debug:
            def f(s):
                s.dma_start(out=dbg["dbg_x1"].rearrange("(k p) t -> p k t", p=128),
                            in_=A[:].rearrange("p (k t) -> p k t", k=KD)
                            ).then_inc(tk.tkd, 16)
                return None
            STEP("sync", f, dma=True, inc=16)

        es_r = contextlib.ExitStack()
        murow = es_r.enter_context(nc.sbuf_tensor("murow", [1, TPC], F32))
        Rrow = es_r.enter_context(nc.sbuf_tensor("Rrow", [1, TPC], F32))
        Irow = es_r.enter_context(nc.sbuf_tensor("Irow", [1, TPC], F32))

        es1 = contextlib.ExitStack()
        x1s = es1.enter_context(nc.sbuf_tensor("x1s", [128, KD * 512], F32))
        for mt in range(NTT // 4):
            ts_ = slice(mt * 512, (mt + 1) * 512)

            def f(v, ts_=ts_):
                return v.tensor_mul(
                    x1s[:].rearrange("p (k t) -> p k t", k=KD),
                    A[:].rearrange("p (k t) -> p k t", k=KD)[:, :, ts_],
                    A[:].rearrange("p (k t) -> p k t", k=KD)[:, :, ts_])
            STEP("vector", f)

            def f(t, ts_=ts_):
                last = None
                for kk in range(KD):
                    last = t.matmul(pE_[0:1, 0:512], ones_colf[:, 0:1],
                                    A[:, kk * TPC:(kk + 1) * TPC][:, ts_],
                                    start=(kk == 0), stop=(kk == KD - 1))
                for gblk in range(4):
                    tt = slice(gblk * 128, (gblk + 1) * 128)
                    for kk in range(KD):
                        last = t.matmul(
                            pF[:, gblk:gblk + 1],
                            A[:, kk * TPC:(kk + 1) * TPC][:, ts_][:, tt],
                            ones_colf[:, 0:1],
                            start=(kk == 0), stop=(kk == KD - 1))
                    for kk in range(KD):
                        last = t.matmul(
                            pF[:, 4 + gblk:4 + gblk + 1],
                            x1s[:, kk * 512:(kk + 1) * 512][:, tt],
                            ones_colf[:, 0:1],
                            start=(kk == 0), stop=(kk == KD - 1))
                return last
            STEP("tensor", f)

            def f(v, mt=mt, ts_=ts_):
                v.tensor_scalar_mul(murow[0:1, ts_], pE_[0:1, 0:512], 1.0 / D)
                cs = slice(mt * 4, (mt + 1) * 4)
                v.tensor_copy(muc2[:, cs], pF[:, 0:4])
                return v.tensor_copy(sxc2[:, cs], pF[:, 4:8])
            STEP("vector", f)
        es1.close()

        def f(v):
            return v.tensor_scalar_mul(vs1[:], muc2[:], 1.0 / D)
        STEP("vector", f)

        def f(v):
            return v.tensor_mul(vs2[:], vs1[:], vs1[:])
        STEP("vector", f)

        def f(v):
            return v.scalar_tensor_tensor(vs2[:], sxc2[:], 1.0 / D, vs2[:],
                                          ALU.mult, ALU.subtract)
        STEP("vector", f)

        def f(v):
            return v.tensor_scalar_add(m8[:], vs2[:], 1.0e-5)
        STEP("vector", f)

        def f(v):
            return v.reciprocal(eidf[:], m8[:])
        STEP("vector", f)

        def f(v):
            return v.tensor_mul(muc2[:], vs2[:], eidf[:])
        STEP("vector", f)

        def f(v):
            return v.tensor_scalar_add(muc2[:], muc2[:], 1.0e-5)
        STEP("vector", f)

        def f(s):
            return s.activation(vs1[:], eidf[:], AF.Sqrt)
        STEP("scalar", f)

        def f(s):
            return s.activation(sxc2[:], muc2[:], AF.Sqrt)
        STEP("scalar", f)

        def f(v):
            return v.reciprocal(vs2[:], sxc2[:])
        STEP("vector", f)

        def f(v):
            return v.tensor_mul(vs2[:], vs1[:], vs2[:])
        STEP("vector", f)

        def f(t):
            t.transpose(pH[0:NTT, 0:128], vs2[:, 0:NTT], eye128f[:])
            return t.transpose(pF[0:NTT, 0:128], sxc2[:, 0:NTT], eye128f[:])
        STEP("tensor", f)

        def f(v):
            v.tensor_copy(keysA[0:NTT, 0:128], pH[0:NTT, 0:128])
            return v.tensor_copy(keysB[0:NTT, 0:128], pF[0:NTT, 0:128])
        STEP("vector", f)

        def f(s):
            s.dma_start(out=rb2[0:TPC].rearrange("(c p) -> c p", p=128),
                        in_=keysA[0:NTT, 0:128]).then_inc(tk.tkd, 16)
            s.dma_start(out=rb2[TPC:2 * TPC].rearrange("(c p) -> c p", p=128),
                        in_=keysB[0:NTT, 0:128]).then_inc(tk.tkd, 16)
            return None
        STEP("sync", f, dma=True, inc=32)

        def f(s):
            s.dma_start(out=Rrow[0:1, :],
                        in_=rb2[0:TPC].rearrange("(a t) -> a t", a=1)
                        ).then_inc(tk.tkd, 16)
            s.dma_start(out=Irow[0:1, :],
                        in_=rb2[TPC:2 * TPC].rearrange("(a t) -> a t", a=1)
                        ).then_inc(tk.tkd, 16)
            return None
        STEP("sync", f, dma=True, inc=32)

        def f(v):
            return v.tensor_mul(murow[0:1, :], murow[0:1, :], Rrow[0:1, :])
        STEP("vector", f)

        es2 = contextlib.ExitStack()
        Bb = es2.enter_context(nc.sbuf_tensor("Bb", [128, KD * TPC], F32))
        for mt in range(NTT // 4):
            ts_ = slice(mt * 512, (mt + 1) * 512)

            def f(t, ts_=ts_):
                t.matmul(pA[:, 0:512], ones_rowf[0:1, 0:128], Rrow[0:1, ts_],
                         start=True, stop=True)
                t.matmul(pB[:, 0:512], ones_rowf[0:1, 0:128], murow[0:1, ts_],
                         start=True, stop=True)
                return t.matmul(pC[:, 0:512], ones_rowf[0:1, 0:128],
                                Irow[0:1, ts_], start=True, stop=True)
            STEP("tensor", f)

            def f(v, ts_=ts_):
                last = None
                for kk in range(KD):
                    asl = A[:, kk * TPC:(kk + 1) * TPC][:, ts_]
                    bsl = Bb[:, kk * TPC:(kk + 1) * TPC][:, ts_]
                    v.tensor_mul(bsl, asl, pA[:, 0:512])
                    v.tensor_sub(bsl, bsl, pB[:, 0:512])
                    v.tensor_mul(pD[:, 0:512], bsl, pC[:, 0:512])
                    v.tensor_add(asl, asl, pD[:, 0:512])
                    v.tensor_single_scalar(bsl, bsl, lnw_c[:, kk:kk + 1],
                                           ALU.mult)
                    last = v.tensor_single_scalar(bsl, bsl, lnb_c[:, kk:kk + 1],
                                                  ALU.add)
                return last
            STEP("vector", f)

        if "ln2" in debug:
            def f(s):
                s.dma_start(out=dbg["dbg_ln2"].rearrange("(k p) t -> p k t", p=128),
                            in_=Bb[:].rearrange("p (k t) -> p k t", k=KD)
                            ).then_inc(tk.tkd, 16)
                return None
            STEP("sync", f, dma=True, inc=16)

        for tt in range(NTT):
            def f(t, tt=tt):
                last = None
                for kk in range(KD):
                    last = t.matmul(
                        pF[:, tt * E:(tt + 1) * E],
                        Bb[:, kk * TPC + tt * 128:kk * TPC + (tt + 1) * 128],
                        rwf[:, kk * E:(kk + 1) * E],
                        start=(kk == 0), stop=(kk == KD - 1))
                return last
            STEP("tensor", f)

        if "logits" in debug:
            def f(v):
                return v.tensor_copy(eqs[:], pF[:, 0:128])
            STEP("vector", f)

            def f(s):
                s.dma_start(out=dbg["dbg_logits"], in_=eqs[:]
                            ).then_inc(tk.tkd, 16)
                return None
            STEP("sync", f, dma=True, inc=16)

        def f(v):
            return v.tensor_reduce(m8[:], pF[:, 0:NTT * E].rearrange(
                "p (t e) -> p t e", t=NTT), axis=AX.X, op=ALU.max)
        STEP("vector", f)

        def f(v):
            last = None
            for tt in range(NTT):
                last = v.tensor_single_scalar(
                    e8[:, tt * E:(tt + 1) * E], pF[:, tt * E:(tt + 1) * E],
                    m8[:, tt:tt + 1], ALU.is_equal)
            return last
        STEP("vector", f)

        def f(s):
            return s.activation(scor[:], m8[:], AF.Sigmoid)
        STEP("scalar", f)

        if "scor" in debug:
            def f(s):
                s.dma_start(out=dbg["dbg_scor"], in_=scor[:]).then_inc(tk.tkd, 16)
                return None
            STEP("sync", f, dma=True, inc=16)

        def f(v):
            return v.tensor_mul(p8[:], e8[:], c8t[:])
        STEP("vector", f)

        def f(v):
            return v.tensor_reduce(vs2[:], p8[:].rearrange(
                "p (t e) -> p t e", t=NTT), axis=AX.X, op=ALU.max)
        STEP("vector", f)

        def f(v):
            return v.tensor_scalar(vs1[:], vs2[:], RND_C, RND_C, ALU.add,
                                   ALU.subtract)
        STEP("vector", f)

        def f(v):
            return v.tensor_scalar(eidf[:], vs1[:], -1.0, float(E), ALU.mult,
                                   ALU.add)
        STEP("vector", f)

        def f(v):
            return v.scalar_tensor_tensor(keysA[:, 0:GB2], eidf[:], float(TPC),
                                          t8til[:], ALU.mult, ALU.add)
        STEP("vector", f)

        if "eid" in debug:
            def f(s):
                s.dma_start(out=dbg["dbg_eid"], in_=eidf[:]).then_inc(tk.tkd, 16)
                return None
            STEP("sync", f, dma=True, inc=16)

        spar = [0]
        emit_bitonic(tk, STEP, nc, [keysA, keysB], spar, pH, eye128f, sgf_s2,
                     1, GB2, nbits2)
        skb = [keysA, keysB][spar[0]]

        if "keys" in debug:
            def f(s):
                s.dma_start(out=dbg["dbg_keys"], in_=skb[:, 0:GB2]
                            ).then_inc(tk.tkd, 16)
                return None
            STEP("sync", f, dma=True, inc=16)

        def f(v):
            return v.tensor_copy(tvals[:], skb[:, 0:GB2])
        STEP("vector", f)

        def f(v):
            return v.tensor_single_scalar(slots[:], tvals[:], EMASK,
                                          ALU.bitwise_and)
        STEP("vector", f)

        def f(v):
            return v.tensor_copy(eids[:], slots[:])
        STEP("vector", f)

        def f(v):
            v.tensor_scalar_mul(eids[:], eids[:], 1.0 / TPC)
            return v.tensor_single_scalar(tvals[:], tvals[:], TPC - 1,
                                          ALU.bitwise_and)
        STEP("vector", f)

        def f(v):
            last = None
            for e in range(E):
                last = v.tensor_single_scalar(
                    eqs[:].rearrange("p (e g) -> p e g", e=E)[:, e],
                    eids[:], float(e), ALU.is_equal)
            return last
        STEP("vector", f)

        def f(t):
            last = None
            for g in range(GB2):
                last = t.matmul(pF[0:E, 0:1],
                                eqs[:].rearrange("p (e g) -> p g e", g=GB2)[:, g],
                                ones_colf[:, 0:1],
                                start=(g == 0), stop=(g == GB2 - 1))
            return last
        STEP("tensor", f)

        def f(v):
            return v.tensor_copy(cnts[0:E, 0:1], pF[0:E, 0:1])
        STEP("vector", f)

        def f(v):
            return v.tensor_single_scalar(ltri[:, 0:E], iota_row[:, 0:E],
                                          iota_col[:, 0:1], ALU.is_gt)
        STEP("vector", f)

        def f(t):
            return t.matmul(pF[0:E, 1:2], ltri[0:E, 0:E], cnts[0:E, 0:1],
                            start=True, stop=True)
        STEP("tensor", f)

        def f(v):
            return v.tensor_copy(startc[0:E, 0:1], pF[0:E, 1:2])
        STEP("vector", f)

        def f(t):
            return t.transpose(pH[0:1, 0:E], startc[0:E, 0:1],
                               eye128f[0:E, 0:E])
        STEP("tensor", f)

        def f(v):
            return v.tensor_copy(rstr[0:1, 0:E], pH[0:1, 0:E])
        STEP("vector", f)

        def f(t):
            return t.matmul(pH[:, 0:E], ones_rowf[0:1, 0:128], rstr[0:1, 0:E],
                            start=True, stop=True)
        STEP("tensor", f)

        def f(v):
            v.tensor_copy(startbc[:, 0:E], pH[:, 0:E])
            return v.memset(startv[:], 0.0)
        STEP("vector", f)

        for e in range(E):
            def f(v, e=e):
                return v.scalar_tensor_tensor(
                    startv[:],
                    eqs[:].rearrange("p (e g) -> p e g", e=E)[:, e],
                    startbc[:, e:e + 1], startv[:], ALU.mult, ALU.add)
            STEP("vector", f)

        def f(v):
            return v.tensor_sub(slotsf[:], t8til[:], startv[:])
        STEP("vector", f)

        def f(v):
            return v.tensor_single_scalar(startv[:], slotsf[:], C - 0.5,
                                          ALU.is_gt)
        STEP("vector", f)

        def f(v):
            return v.scalar_tensor_tensor(slotsf[:], eids[:], float(C),
                                          slotsf[:], ALU.mult, ALU.add)
        STEP("vector", f)

        def f(v):
            return v.scalar_tensor_tensor(slotsf[:], startv[:], 1.0e6,
                                          slotsf[:], ALU.mult, ALU.add)
        STEP("vector", f)

        def f(v):
            return v.tensor_copy(slots[:], slotsf[:])
        STEP("vector", f)

        def f(gq):
            for g in range(GB2):
                gq.indirect_dma_start(
                    gidx.rearrange("(n o) -> n o", o=1),
                    IndirectOffsetOnAxis(slots[:, g:g + 1], 0),
                    tvals[:, g:g + 1], None,
                    bounds_check=regs["bcG"], oob_is_err=False,
                ).then_inc(tk.tkg, 16)
            return None
        STEP("gpsimd", f, dma=True, inc=16 * GB2)

        if "gidx" in debug:
            def f(s):
                s.dma_start(out=dbg["dbg_gidx"], in_=gidx).then_inc(tk.tkd, 16)
                return None
            STEP("sync", f, dma=True, inc=16)

        def f(s):
            s.dma_start(out=idxcol[:],
                        in_=gidx.rearrange("(c p) -> p c", p=128),
                        allow_slow_non_contiguous=True).then_inc(tk.tkd, 16)
            return None
        STEP("sync", f, dma=True, inc=16)

        def f(v):
            return v.tensor_copy(ln2b[:], Bb[:])
        STEP("vector", f)
        es2.close()
        es_r.close()

        es4 = contextlib.ExitStack()
        tstage = es4.enter_context(nc.sbuf_tensor("tstage", [128, ROW2], BF16))
        for tt in range(NTT):
            for k0 in (0, 4):
                def f(t, tt=tt, k0=k0):
                    last = None
                    for kk in range(k0, k0 + 4):
                        last = t.transpose(
                            pG[:, (kk - k0) * 128:(kk - k0 + 1) * 128],
                            ln2b[:, kk * TPC + tt * 128:kk * TPC + (tt + 1) * 128],
                            eye128[:])
                    return last
                STEP("tensor", f)

                def f(v, tt=tt, k0=k0):
                    last = None
                    for kk in range(k0, k0 + 4):
                        last = v.tensor_single_scalar(
                            tstage[:, kk * 128:(kk + 1) * 128],
                            pG[:, (kk - k0) * 128:(kk - k0 + 1) * 128],
                            scor[:, tt:tt + 1], ALU.mult)
                    return last
                STEP("vector", f)

            def f(s, tt=tt):
                s.dma_start(out=xtab[tt * 128:(tt + 1) * 128, :],
                            in_=tstage[:]).then_inc(tk.tkd, 16)
                return None
            STEP("sync", f, dma=True, inc=16)
        es4.close()

        es3 = contextlib.ExitStack()
        wstr = es3.enter_context(nc.sbuf_tensor("wstr", [128, KD * 1024], BF16))
        hg = es3.enter_context(nc.sbuf_tensor("hg", [128, KI * C], BF16))
        xe = es3.enter_context(nc.sbuf_tensor("xe", [128, CG * ROW2], BF16))
        xef = es3.enter_context(nc.sbuf_tensor("xef", [128, KD * C], BF16))
        scst2 = es3.enter_context(nc.sbuf_tensor("scst2", [128, CG * D], BF16))
        scr5 = es3.enter_context(nc.sbuf_tensor("scr5", [128, C], BF16))
        motile = es3.enter_context(nc.sbuf_tensor("motile", [128, KD * 128], BF16))

        HIW = 512
        NQ = I // HIW
        KQ = HIW // 128
        for ihq in range(NQ):
            def f(g, ihq=ihq):
                g.dma_start(
                    out=wstr[:, 0:KD * HIW].rearrange("p (k m) -> p k m", k=KD),
                    in_=shg_i.rearrange("(k p) m -> p k m", p=128)
                    [:, :, ihq * HIW:(ihq + 1) * HIW]).then_inc(tk.tkg, 16)
                return None
            STEP("gpsimd", f, dma=True, inc=16)

            for m in range(KQ):
                for n4 in range(NTT // 4):
                    def f(t, m=m, n4=n4):
                        ts_ = slice(n4 * 512, (n4 + 1) * 512)
                        last = None
                        for kk in range(KD):
                            last = t.matmul(
                                pA[:, 0:512],
                                wstr[:, kk * HIW + m * 128:
                                     kk * HIW + (m + 1) * 128],
                                ln2b[:, kk * TPC:(kk + 1) * TPC][:, ts_],
                                start=(kk == 0), stop=(kk == KD - 1))
                        return last
                    STEP("tensor", f)

                    def f(s, m=m, n4=n4):
                        return s.activation(
                            hg[:, (m * 4 + n4) * 512:(m * 4 + n4 + 1) * 512],
                            pA[:, 0:512], AF.Silu)
                    STEP("scalar", f)

            def f(g, ihq=ihq):
                g.dma_start(
                    out=wstr[:, 0:KD * HIW].rearrange("p (k m) -> p k m", k=KD),
                    in_=shu_i.rearrange("(k p) m -> p k m", p=128)
                    [:, :, ihq * HIW:(ihq + 1) * HIW]).then_inc(tk.tkg, 16)
                return None
            STEP("gpsimd", f, dma=True, inc=16)

            for m in range(KQ):
                for n4 in range(NTT // 4):
                    def f(t, m=m, n4=n4):
                        ts_ = slice(n4 * 512, (n4 + 1) * 512)
                        last = None
                        for kk in range(KD):
                            last = t.matmul(
                                pA[:, 0:512],
                                wstr[:, kk * HIW + m * 128:
                                     kk * HIW + (m + 1) * 128],
                                ln2b[:, kk * TPC:(kk + 1) * TPC][:, ts_],
                                start=(kk == 0), stop=(kk == KD - 1))
                        return last
                    STEP("tensor", f)

                    def f(v, m=m, n4=n4):
                        hsl = hg[:, (m * 4 + n4) * 512:(m * 4 + n4 + 1) * 512]
                        return v.tensor_mul(hsl, hsl, pA[:, 0:512])
                    STEP("vector", f)

            def f(g, ihq=ihq):
                g.dma_start(
                    out=wstr[:, 0:KQ * D].rearrange("p (k m) -> p k m", m=D),
                    in_=shd_i[ihq * HIW:(ihq + 1) * HIW, :].rearrange(
                        "(k p) m -> p k m", p=128)).then_inc(tk.tkg, 16)
                return None
            STEP("gpsimd", f, dma=True, inc=16)

            for mo in range(KD):
                for n4 in range(NTT // 4):
                    def f(t, mo=mo, n4=n4):
                        last = None
                        for ki in range(KQ):
                            last = t.matmul(
                                pB[:, 0:512],
                                wstr[:, ki * D + mo * 128:ki * D + (mo + 1) * 128],
                                hg[:, (ki * 4 + n4) * 512:(ki * 4 + n4 + 1) * 512],
                                start=(ki == 0), stop=(ki == KQ - 1))
                        return last
                    STEP("tensor", f)

                    def f(v, mo=mo, n4=n4):
                        asl = A[:, mo * TPC + n4 * 512:mo * TPC + (n4 + 1) * 512]
                        return v.tensor_add(asl, asl, pB[:, 0:512])
                    STEP("vector", f)

        for e in range(E):
            def f(v):
                return v.memset(xe[:], 0.0)
            STEP("vector", f)

            def f(gq, e=e):
                for g in range(CG):
                    gq.indirect_dma_start(
                        xe[:, g * ROW2:(g + 1) * ROW2], None, xtab,
                        IndirectOffsetOnAxis(
                            idxcol[:, e * CG + g:e * CG + g + 1], 0),
                        bounds_check=regs["bcT"], oob_is_err=False,
                    ).then_inc(tk.tkg, 16)
                return None
            STEP("gpsimd", f, dma=True, inc=16 * CG)

            for g in range(CG):
                for k0 in (0, 4):
                    def f(t, g=g, k0=k0):
                        last = None
                        for kk in range(k0, k0 + 4):
                            last = t.transpose(
                                pG[:, (kk - k0) * 128:(kk - k0 + 1) * 128],
                                xe[:, g * ROW2 + kk * 128:g * ROW2 + (kk + 1) * 128],
                                eye128[:])
                        return last
                    STEP("tensor", f)

                    def f(v, g=g, k0=k0):
                        last = None
                        for kk in range(k0, k0 + 4):
                            last = v.tensor_copy(
                                xef[:, kk * C + g * 128:kk * C + (g + 1) * 128],
                                pG[:, (kk - k0) * 128:(kk - k0 + 1) * 128])
                        return last
                    STEP("vector", f)

            for ih2 in range(2):
                def f(g, e=e, ih2=ih2):
                    g.dma_start(
                        out=wstr[:].rearrange("p (k m) -> p k m", k=KD),
                        in_=gw_i[e].rearrange("(k p) m -> p k m", p=128)
                        [:, :, ih2 * 1024:(ih2 + 1) * 1024]).then_inc(tk.tkg, 16)
                    return None
                STEP("gpsimd", f, dma=True, inc=16)

                for m in range(KD):
                    for c0, c1 in ((0, 512), (512, C)):
                        def f(t, m=m, c0=c0, c1=c1):
                            last = None
                            for kk in range(KD):
                                last = t.matmul(
                                    pA[:, 0:c1 - c0],
                                    wstr[:, kk * 1024 + m * 128:
                                         kk * 1024 + (m + 1) * 128],
                                    xef[:, kk * C + c0:kk * C + c1],
                                    start=(kk == 0), stop=(kk == KD - 1))
                            return last
                        STEP("tensor", f)

                        def f(s, m=m, ih2=ih2, c0=c0, c1=c1):
                            return s.activation(
                                hg[:, (ih2 * 8 + m) * C + c0:
                                   (ih2 * 8 + m) * C + c1],
                                pA[:, 0:c1 - c0], AF.Silu)
                        STEP("scalar", f)

                def f(g, e=e, ih2=ih2):
                    g.dma_start(
                        out=wstr[:].rearrange("p (k m) -> p k m", k=KD),
                        in_=uw_i[e].rearrange("(k p) m -> p k m", p=128)
                        [:, :, ih2 * 1024:(ih2 + 1) * 1024]).then_inc(tk.tkg, 16)
                    return None
                STEP("gpsimd", f, dma=True, inc=16)

                for m in range(KD):
                    for c0, c1 in ((0, 512), (512, C)):
                        def f(t, m=m, c0=c0, c1=c1):
                            last = None
                            for kk in range(KD):
                                last = t.matmul(
                                    pA[:, 0:c1 - c0],
                                    wstr[:, kk * 1024 + m * 128:
                                         kk * 1024 + (m + 1) * 128],
                                    xef[:, kk * C + c0:kk * C + c1],
                                    start=(kk == 0), stop=(kk == KD - 1))
                            return last
                        STEP("tensor", f)

                        def f(v, m=m, ih2=ih2, c0=c0, c1=c1):
                            hsl = hg[:, (ih2 * 8 + m) * C + c0:
                                     (ih2 * 8 + m) * C + c1]
                            return v.tensor_mul(hsl, hsl, pA[:, 0:c1 - c0])
                        STEP("vector", f)

            for mo in range(KD):
                def f(g, e=e, mo=mo):
                    g.dma_start(
                        out=wstr[:, 0:KI * 128].rearrange(
                            "p (k m) -> p k m", m=128),
                        in_=dw_i[e][:, mo * 128:(mo + 1) * 128].rearrange(
                            "(k p) m -> p k m", p=128)).then_inc(tk.tkg, 16)
                    return None
                STEP("gpsimd", f, dma=True, inc=16)

                for c0, c1 in ((0, 512), (512, C)):
                    def f(t, mo=mo, c0=c0, c1=c1):
                        last = None
                        for ki in range(KI):
                            last = t.matmul(
                                pB[:, 0:c1 - c0],
                                wstr[:, ki * 128:(ki + 1) * 128],
                                hg[:, ki * C + c0:ki * C + c1],
                                start=(ki == 0), stop=(ki == KI - 1))
                        return last
                    STEP("tensor", f)

                    def f(v, mo=mo, c0=c0, c1=c1):
                        return v.tensor_copy(scr5[:, c0:c1], pB[:, 0:c1 - c0])
                    STEP("vector", f)

                for g0 in range(0, CG, 4):
                    g1 = min(g0 + 4, CG)

                    def f(t, g0=g0, g1=g1):
                        last = None
                        for g in range(g0, g1):
                            last = t.transpose(
                                pG[:, (g - g0) * 128:(g - g0 + 1) * 128],
                                scr5[:, g * 128:(g + 1) * 128], eye128[:])
                        return last
                    STEP("tensor", f)

                    def f(v, mo=mo, g0=g0, g1=g1):
                        last = None
                        for g in range(g0, g1):
                            last = v.tensor_copy(
                                scst2[:, g * D + mo * 128:g * D + (mo + 1) * 128],
                                pG[:, (g - g0) * 128:(g - g0 + 1) * 128])
                        return last
                    STEP("vector", f)

            def f(gq, e=e):
                for g in range(CG):
                    gq.indirect_dma_start(
                        moetmp,
                        IndirectOffsetOnAxis(
                            idxcol[:, e * CG + g:e * CG + g + 1], 0),
                        scst2[:, g * D:(g + 1) * D], None,
                        bounds_check=regs["bcT"], oob_is_err=False,
                    ).then_inc(tk.tkg, 16)
                return None
            STEP("gpsimd", f, dma=True, inc=16 * CG)

        if "moe" in debug:
            def f(s):
                s.dma_start(out=dbg["dbg_moe"], in_=moetmp).then_inc(tk.tkd, 16)
                return None
            STEP("sync", f, dma=True, inc=16)

        for tt in range(NTT):
            def f(s, tt=tt):
                s.dma_start(out=motile[:].rearrange("p (k m) -> p k m", m=128),
                            in_=moetmp[tt * 128:(tt + 1) * 128, :].rearrange(
                                "p (k m) -> p k m", m=128)).then_inc(tk.tkd, 16)
                return None
            STEP("sync", f, dma=True, inc=16)

            for k0 in (0, 4):
                def f(t, k0=k0):
                    last = None
                    for kk in range(k0, k0 + 4):
                        last = t.transpose(
                            pG[:, (kk - k0) * 128:(kk - k0 + 1) * 128],
                            motile[:, kk * 128:(kk + 1) * 128], eye128[:])
                    return last
                STEP("tensor", f)

                def f(v, tt=tt, k0=k0):
                    last = None
                    for kk in range(k0, k0 + 4):
                        asl = A[:, kk * TPC + tt * 128:kk * TPC + (tt + 1) * 128]
                        last = v.tensor_add(
                            asl, asl, pG[:, (kk - k0) * 128:(kk - k0 + 1) * 128])
                    return last
                STEP("vector", f)
        es3.close()

        def f(s):
            s.dma_start(out=out_o.rearrange("(k p) t -> p k t", p=128),
                        in_=A[:].rearrange("p (k t) -> p k t", k=KD)
                        ).then_inc(tk.tkd, 16)
            return None
        STEP("sync", f, dma=True, inc=16)

        wc, wd, wg = tk.barrier_values()

        def fbar(s):
            s.wait_ge(tk.tkc, wc)
            s.wait_ge(tk.tkd, wd)
            s.wait_ge(tk.tkg, wg)
            return None
        tk.steps["sync"].append(lambda engine: fbar(engine))

        with nc.Block() as block:
            tk.run(block)

    return nc, dbg


def _rot_host(cfg: Cfg):
    import jax
    import jax.numpy as jnp
    cpu = jax.devices("cpu")[0]
    with jax.default_device(cpu):
        hk = jax.random.key(42)
        rot = np.asarray(jax.random.normal(
            hk, (cfg.NH, cfg.DH, cfg.NHASH, cfg.NB), jnp.float32))
    return rot


def _sign_tiles(cfg: Cfg):
    nb = cfg.nbits
    GB = cfg.GB
    out = np.zeros((nb - 1, 128, 128), np.float32)
    for p in range(1, nb):
        P = np.arange(128)
        gmat = np.zeros((128, 128), np.float32)
        for half in range(128 // GB if GB <= 64 else 1):
            for g in range(GB):
                i = g * 128 + P
                gmat[half * GB + g, :] = 1.0 - 2.0 * ((i >> p) & 1)
        out[p - 1] = gmat
    return out


def run_attn(cfg: Cfg, inputs, debug=(), n_cores=None, sim=False, sort_trunc=None, stop_after_sort=False):
    x = np.asarray(inputs["L0_attn_x"], np.float32)
    w_qk = np.asarray(inputs["L0_attn_w_qk"], np.float32)
    w_v = np.asarray(inputs["L0_attn_w_v"], np.float32)
    rot = _rot_host(cfg)
    sg = _sign_tiles(cfg)

    xfm = np.ascontiguousarray(x.reshape(cfg.T, cfg.D).T)
    ncares = cfg.ncores if n_cores is None else n_cores
    maps = []
    for c in range(cfg.ncores):
        heads = list(range(cfg.HPC * c, cfg.HPC * (c + 1)))
        m = {
            "xf": xfm,
            "wqkT": np.ascontiguousarray(
                w_qk[heads[0] * cfg.DH:(heads[-1] + 1) * cfg.DH, :].T),
            "wvT": np.ascontiguousarray(
                w_v[heads[0] * cfg.DH:(heads[-1] + 1) * cfg.DH, :].T),
            "sgf": sg,
        }
        for j, h in enumerate(heads):
            re = rot[h].reshape(cfg.DH, cfg.NHASH * cfg.NB)
            m[f"rot{j}"] = np.ascontiguousarray(
                np.concatenate([re, re], axis=0))
        maps.append(m)

    nc, dbg = build_attn(cfg, debug=debug, sort_trunc=sort_trunc, stop_after_sort=stop_after_sort)
    if sim:
        from concourse.bass_interp import MultiCoreSim
        from concourse.libnrt import get_pod_config
        simu = MultiCoreSim(nc, cfg.ncores, **get_pod_config())
        for c in range(cfg.ncores):
            for k, v in maps[c].items():
                simu.cores[c].tensor(k)[:] = v
        simu.simulate()
        outs = []
        names = ["H"] + list(dbg.keys())
        for c in range(cfg.ncores):
            outs.append({nm: np.array(simu.cores[c].tensor(nm)) for nm in names})
        return outs
    res = run_bass_kernel_spmd(nc, maps, core_ids=list(range(cfg.ncores)))
    return res.results


def _sign_tiles_g(nbits, GB):
    out = np.zeros((nbits - 1, 128, 128), np.float32)
    P = np.arange(128)
    for p in range(1, nbits):
        gmat = np.zeros((128, 128), np.float32)
        for g in range(128):
            i = g * 128 + P
            gmat[g, :] = 1.0 - 2.0 * ((i >> p) & 1)
        out[p - 1] = gmat
    return out


def _stage_ffn(cfg: Cfg, inputs, Afm, xfm):
    E, I, C = 8, 2048, MOE_C
    w_out = np.asarray(inputs["L0_attn_w_out"], np.float32)
    shared = {
        "woT": np.ascontiguousarray(w_out.T),
        "lnw": np.asarray(inputs["L0_ffn_ln_w"], np.float32),
        "lnb": np.asarray(inputs["L0_ffn_ln_b"], np.float32),
        "rw": np.asarray(inputs["L0_ffn_router_w"], np.float32),
        "gw": np.asarray(inputs["L0_ffn_gate_w"], np.float32),
        "uw": np.asarray(inputs["L0_ffn_up_w"], np.float32),
        "dw": np.asarray(inputs["L0_ffn_down_w"], np.float32),
        "shg": np.asarray(inputs["L0_ffn_sh_gate"], np.float32),
        "shu": np.asarray(inputs["L0_ffn_sh_up"], np.float32),
        "shd": np.asarray(inputs["L0_ffn_sh_down"], np.float32),
        "sgf2": _sign_tiles_g(int(math.log2(cfg.TPC)), cfg.TPC // 128),
        "gidx0": np.full(E * C, cfg.TPC, np.int32),
    }
    maps = []
    for c in range(cfg.ncores):
        sl = slice(c * cfg.TPC, (c + 1) * cfg.TPC)
        m = dict(shared)
        m["xsl"] = np.ascontiguousarray(xfm[:, sl])
        m["Hfm"] = np.ascontiguousarray(Afm[:, sl])
        maps.append(m)
    return maps


def run_ffn(cfg: Cfg, maps, debug=(), sim=False):
    nc, dbg = build_ffn(cfg, debug=debug)
    if sim:
        from concourse.bass_interp import MultiCoreSim
        from concourse.libnrt import get_pod_config
        ncores = len(maps)
        simu = MultiCoreSim(nc, ncores, **get_pod_config())
        for c in range(ncores):
            for k, v in maps[c].items():
                simu.cores[c].tensor(k)[:] = v
        simu.simulate()
        outs = []
        names = ["out"] + list(dbg.keys())
        for c in range(ncores):
            outs.append({nm: np.array(simu.cores[c].tensor(nm)) for nm in names})
        return outs
    res = run_bass_kernel_spmd(nc, maps, core_ids=list(range(len(maps))))
    return res.results


def kernel(**inputs):
    cfg = Cfg()
    D, T, TPC, S, DH = cfg.D, cfg.T, cfg.TPC, cfg.S, cfg.DH

    outsA = run_attn(cfg, inputs, debug=(), sim=False)
    Afm = np.empty((D, T), np.float32)
    for ca in range(cfg.ncores):
        Hc = np.asarray(outsA[ca]["H"], np.float32)
        for u in range(cfg.UPC):
            b, j = u // cfg.HPC, u % cfg.HPC
            h = cfg.HPC * ca + j
            Afm[h * DH:(h + 1) * DH, b * S:(b + 1) * S] = Hc[u].T

    x = np.asarray(inputs["L0_attn_x"], np.float32)
    xfm = np.ascontiguousarray(x.reshape(T, D).T)
    maps = _stage_ffn(cfg, inputs, Afm, xfm)
    outsB = run_ffn(cfg, maps, sim=False)

    xout = np.empty((T, D), np.float32)
    for c in range(cfg.ncores):
        xout[c * TPC:(c + 1) * TPC, :] = np.asarray(outsB[c]["out"]).T
    return xout.reshape(cfg.B, S, D)
